# version 16
# baseline (speedup 1.0000x reference)
import numpy as np
import concourse.bass as bass
import concourse.mybir as mybir
from concourse.ap import AP
from concourse.bass_utils import run_bass_kernel_spmd

F32 = mybir.dt.float32
BF16 = mybir.dt.bfloat16
AF = mybir.ActivationFunctionType
ALU = mybir.AluOpType
AX = mybir.AxisListType

D = 1024
TOK = 2048
NT = 16
KC = 8
NE = 8
EPS = 1e-10
LNEPS = 1e-5
BIG = 1e30
NCORES = 8

ENGINES = ("tensor", "vector", "scalar", "gpsimd", "sync")


class Prog:

    def __init__(self, nc, es):
        self.nc = nc
        self.es = es
        self.ops = {e: [] for e in ENGINES}
        self.cnt = {}
        self.semh = {}

    def sem(self, name):
        self.semh[name] = self.es.enter_context(self.nc.semaphore(name))
        self.cnt[name] = 0

    def on(self, eng, fn, waits=(), inc=None, amt=1, drain=False):
        tok = None
        if inc is not None:
            self.cnt[inc] += amt
            tok = (inc, self.cnt[inc])
        self.ops[eng].append((fn, tuple(w for w in waits if w is not None), inc, amt, drain))
        return tok

    def dma(self, eng, out, in_, sem, waits=(), **kw):
        return self.on(eng, lambda e, o=out, i=in_, k=kw: e.dma_start(out=o, in_=i, **k),
                       waits, inc=sem, amt=16)

    def emit(self, block):
        for ename in ENGINES:
            ops = self.ops[ename]
            semh = self.semh

            def body(eng, ops=ops):
                hw = {}
                for fn, waits, inc, amt, drain in ops:
                    if drain:
                        eng.drain()
                    byname = {}
                    for (s, v) in waits:
                        byname[s] = max(byname.get(s, 0), v)
                    for s, v in byname.items():
                        if hw.get(s, 0) < v:
                            eng.wait_ge(semh[s], v)
                            hw[s] = v
                    inst = fn(eng)
                    if inc is not None:
                        inst.then_inc(semh[inc], amt)
            getattr(block, ename)(body)


def build():
    nc = bass.Bass("TRN2", target_bir_lowering=False, debug=False)
    x = nc.dram_tensor("x", [TOK, D], F32, kind="ExternalInput")
    wq = nc.dram_tensor("wq", [D, D], F32, kind="ExternalInput")
    wk = nc.dram_tensor("wk", [D, D], F32, kind="ExternalInput")
    wv = nc.dram_tensor("wv", [D, D], F32, kind="ExternalInput")
    bq = nc.dram_tensor("bq", [1, D], F32, kind="ExternalInput")
    bk = nc.dram_tensor("bk", [1, D], F32, kind="ExternalInput")
    bv = nc.dram_tensor("bv", [1, D], F32, kind="ExternalInput")
    rw = nc.dram_tensor("rw", [NE, D], F32, kind="ExternalInput")
    rb = nc.dram_tensor("rb", [1, NE], F32, kind="ExternalInput")
    ew = nc.dram_tensor("ew", [NE, D, D], F32, kind="ExternalInput")
    out = nc.dram_tensor("out", [TOK, D], F32, kind="ExternalOutput")
    M2 = nc.dram_tensor("M2", [D, 2 * D], BF16)
    cc_in = nc.dram_tensor("cc_in", [1, D], F32)
    cc_out = nc.dram_tensor("cc_out", [1, D], F32)
    sv2 = nc.dram_tensor("sv2", [1, 3 * D], BF16)

    from contextlib import ExitStack
    es = ExitStack()
    sb = lambda n, s, dt: es.enter_context(nc.sbuf_tensor(n, s, dt))
    psal = lambda n, s, dt: es.enter_context(nc.psum_tensor(n, s, dt))
    with es:
        R = sb("R", [128, NT * D], F32)
        XnT = sb("XnT", [128, KC * TOK], BF16)
        WA = sb("WA", [128, 2 * KC * D], BF16)
        WB = sb("WB", [128, 2 * KC * D], BF16)
        QT = sb("QT", [128, 2 * KC * D], BF16)
        xnscr = sb("xnscr", [128, 4 * D], BF16)
        NSLOT = 4
        idf = sb("idf", [8, 8], F32)
        idb = sb("idb", [128, 128], BF16)
        ones1b = sb("ones1b", [1, 128], BF16)
        onescol = sb("onescol", [128, 1], BF16)
        WrT = sb("WrT", [128, KC * NE], BF16)
        rbf = sb("rbf", [1, NE], F32)
        rbb = sb("rbb", [1, NE], BF16)
        bqT = sb("bqT", [128, 8], F32)
        bkr = sb("bkr", [1, D], BF16)
        bvr = sb("bvr", [1, D], BF16)
        ST = sb("ST", [128, 256], F32)
        logits = sb("logits", [128, NT * NE], F32)
        pm = sb("pm", [128, NT * NE], F32)
        coef = sb("coef", [128, NT * NE], F32)
        msk = sb("msk", [128, 64], F32)
        PA = psal("PA", [128, 2048], F32)
        PB = psal("PB", [128, 2048], F32)

        def st(i):
            return ST.ap()[:, i * 16:(i + 1) * 16]
        SSUM, SSQ, MU, VAR, RSTD, NEGMUR, M1, NEGM1, M2c, Zc, TA, TB, TC, Gc, E2, RZ = \
            [st(i) for i in range(16)]

        def stc(sl_, t):
            return sl_[:, t:t + 1]

        P = Prog(nc, es)
        for s in ("T", "V", "S", "G", "DX0", "DX1", "DX2", "DX3", "DX4", "DX5", "DX6", "DX7", "DRW", "DRB",
                  "DBQ", "DW0", "DW1", "DK", "DQ0", "DQ1", "DS", "DM0", "DM1", "DM2", "DM3", "DC", "DC2", "CC", "DOUT"):
            P.sem(s)

        block = es.enter_context(nc.Block())

        psum_user = [None] * 8

        def banks_of(tensor, c0, c1):
            base = 0 if tensor is PA else 4
            return list(range(base + c0 // 512, base + (c1 + 511) // 512))

        def ps_wait(tensor, c0, c1):
            return [psum_user[b] for b in banks_of(tensor, c0, c1)]

        def ps_set(tensor, c0, c1, tok):
            for b in banks_of(tensor, c0, c1):
                psum_user[b] = tok

        XnT3 = XnT.ap().rearrange("p (k c) -> p k c", k=KC)
        Rt = lambda t: R.ap()[:, t * D:(t + 1) * D]
        PAb = PA.ap().bitcast(BF16)
        PBb = PB.ap().bitcast(BF16)
        xn_f32 = xnscr.ap().bitcast(F32)

        x3 = x.ap().rearrange("(t p) d -> p t d", p=128)
        R3 = R.ap().rearrange("p (t d) -> p t d", t=NT)
        XnTf = XnT.ap().bitcast(F32)
        land_rw = XnTf[0:8, 0:D]
        t_rw = P.dma("sync", land_rw, rw.ap(), "DRW")
        t_rb = P.dma("sync", rbf[:, :], rb.ap(), "DRB")
        QTf0 = QT.ap().bitcast(F32)
        t_bq = P.dma("scalar", QTf0[0:1, 0:D], bq.ap(), "DBQ")
        t_bk = P.dma("scalar", QTf0[0:1, D:2 * D], bk.ap(), "DBQ")
        t_bv = P.dma("scalar", QTf0[0:1, 2 * D:3 * D], bv.ap(), "DBQ")
        t_xq = [P.dma("sync" if q % 2 == 0 else "scalar",
                      R3[:, q * 2:(q + 1) * 2, :], x3[:, q * 2:(q + 1) * 2, :], "DX%d" % q,
                      waits=())
                for q in range(8)]
        t_ones1 = P.on("gpsimd", lambda e: e.memset(ones1b[:, :], 1.0), inc="G")
        t_onesc = P.on("gpsimd", lambda e: e.memset(onescol[:, :], 1.0), inc="G")
        t_idsrc = P.on("gpsimd", lambda e: e.memset(xn_f32[:, 0:128], 1.0), inc="G")
        t_idsrcb = P.on("gpsimd", lambda e: e.memset(xnscr.ap()[:, 0:128], 1.0), inc="G", drain=True)
        t_idf = P.on("gpsimd", lambda e: e.affine_select(
            out=idf[:, :], in_=xn_f32[0:8, 64:72], pattern=[[-1, 8]],
            compare_op=ALU.is_equal, fill=0.0, base=0, channel_multiplier=1),
            waits=[t_idsrc], inc="G", drain=True)
        t_idb0 = P.on("gpsimd", lambda e: e.memset(idb[:, :], 0.0), inc="G")
        t_idb = P.on("gpsimd", lambda e: e.affine_select(
            out=idb[:, :], in_=xnscr.ap()[:, 0:128], pattern=[[-1, 128]],
            compare_op=ALU.is_equal, fill=0.0, base=0, channel_multiplier=1),
            waits=[t_idb0, t_idsrcb], inc="G", drain=True)
        xnscr_reader = [t_idb, t_idb, t_idb, t_idb]
        xnscr_writer = [t_idsrcb, t_idsrcb, t_idsrcb, t_idsrcb]

        def ln_stats_tile(t, r_dep, extra=None):
            slot = t % NSLOT
            xs = xnscr.ap()[:, slot * D:(slot + 1) * D]
            sq = P.on("scalar", lambda e, xs=xs, t=t: e.activation(
                out=xs, in_=Rt(t), func=AF.Square, accum_out=stc(SSQ, t)),
                waits=[r_dep, xnscr_reader[slot], xnscr_writer[slot], extra], inc="S")
            xnscr_writer[slot] = sq
            xnscr_reader[slot] = None
            sm = P.on("vector", lambda e, t=t: e.reduce_sum(
                out=stc(SSUM, t), in_=Rt(t), axis=AX.X),
                waits=[r_dep], inc="V")
            return sq, sm

        def ln_batch_half(hh, stat_toks):
            sl = lambda S: S[:, hh * 8:hh * 8 + 8]
            v = lambda fn, waits=(), drain=True: P.on("vector", fn, waits, inc="V", drain=drain)
            t1 = v(lambda e, sl=sl: e.tensor_scalar_mul(out=sl(TA), in0=sl(SSQ), scalar1=1.0 / D),
                   stat_toks, drain=False)
            t2 = v(lambda e, sl=sl: e.tensor_scalar_mul(out=sl(MU), in0=sl(SSUM), scalar1=1.0 / D))
            t3 = v(lambda e, sl=sl: e.tensor_mul(out=sl(TB), in0=sl(MU), in1=sl(MU)), [t2])
            t4 = v(lambda e, sl=sl: e.tensor_sub(out=sl(VAR), in0=sl(TA), in1=sl(TB)), [t1, t3])
            t5 = v(lambda e, sl=sl: e.tensor_scalar_add(out=sl(VAR), in0=sl(VAR), scalar1=LNEPS), [t4])
            t6 = P.on("scalar", lambda e, sl=sl: e.activation(out=sl(TA), in_=sl(VAR), func=AF.Ln),
                      waits=[t5], inc="S")
            pr = P.on("scalar", lambda e, sl=sl: e.activation(out=sl(RSTD), in_=sl(TA),
                                                              func=AF.Exp, scale=-0.5),
                      waits=[t6], inc="S", drain=True)
            m1t = v(lambda e, sl=sl: e.tensor_mul(out=sl(TA), in0=sl(MU), in1=sl(RSTD)), [pr])
            t_nm = v(lambda e, sl=sl: e.tensor_scalar_mul(out=sl(NEGMUR), in0=sl(TA), scalar1=-1.0), [m1t])
            return t_nm

        def ln_norm_tile(t, bm_tok, xnt_war, tr_ps, tr_psb):
            slot = t % NSLOT
            xs = xnscr.ap()[:, slot * D:(slot + 1) * D]
            nrm = P.on("scalar", lambda e, xs=xs, t=t: e.activation(
                out=xs, in_=Rt(t), func=AF.Identity, bias=stc(NEGMUR, t), scale=stc(RSTD, t)),
                waits=[bm_tok, xnscr_reader[slot], xnscr_writer[slot]], inc="S")
            xnscr_writer[slot] = nrm
            bf0 = (t % 2) * 1024
            trs = None
            for j in range(KC):
                w = [nrm, t_idb] + (ps_wait(tr_ps, (t % 2) * 512, (t % 2 + 1) * 512) if j == 0 else [])
                if j == 0 and xnt_war is not None:
                    w.append(xnt_war)
                trs = P.on("tensor", lambda e, xs=xs, j=j, bf0=bf0, tr_psb=tr_psb: e.transpose(
                    tr_psb[:, bf0 + j * 128:bf0 + (j + 1) * 128],
                    xs[:, j * 128:(j + 1) * 128], idb[:, :]),
                    waits=w, inc=("T" if j == KC - 1 else None))
            xnscr_reader[slot] = trs
            evtok = P.on("vector", lambda e, t=t, bf0=bf0, tr_psb=tr_psb: e.tensor_copy(
                out=XnT3[:, :, t * 128:(t + 1) * 128],
                in_=tr_psb[:, bf0:bf0 + 1024].rearrange("p (j c) -> p j c", j=8)),
                waits=[trs], inc="V")
            ps_set(tr_ps, (t % 2) * 512, (t % 2 + 1) * 512, evtok)
            return evtok

        def router_tile(t, ev_t):
            pcol = (t % 2) * 512
            pslot = PB.ap()[:, pcol:pcol + 8]
            for k in range(KC):
                w = ([ev_t, t_wrt, t_rbb, t_ones1] + ps_wait(PB, pcol, pcol + 8)) if k == 0 else []
                P.on("tensor", lambda e, k=k, t=t, pslot=pslot: e.matmul(
                    pslot, XnT.ap()[:, k * TOK + t * 128:k * TOK + (t + 1) * 128],
                    WrT[:, k * NE:(k + 1) * NE], start=(k == 0), stop=False), waits=w)
            rt = P.on("tensor", lambda e, pslot=pslot: e.matmul(
                pslot, ones1b[0:1, :], rbb[0:1, :], start=False, stop=True), inc="T")
            lg = P.on("vector", lambda e, t=t, pslot=pslot: e.tensor_copy(
                out=logits.ap()[:, t * NE:(t + 1) * NE], in_=pslot),
                waits=[rt], inc="V")
            ps_set(PB, pcol, pcol + 8, lg)
            return lg

        def coef_half(hh, lg):
            tl = list(range(hh * 8, hh * 8 + 8))
            sl = lambda S: S[:, hh * 8:hh * 8 + 8]
            ct = {}
            m1l = None
            for t in tl:
                lt = logits.ap()[:, t * NE:(t + 1) * NE]
                m1l = P.on("vector", lambda e, t=t, lt=lt: e.reduce_max(
                    out=stc(M1, t), in_=lt, axis=AX.X), waits=[lg[t]],
                    inc=("V" if t == tl[-1] else None))
            t_nm1 = P.on("vector", lambda e, sl=sl: e.tensor_scalar_mul(
                out=sl(NEGM1), in0=sl(M1), scalar1=-1.0), waits=[m1l], inc="V", drain=True)
            exp_toks = {}
            pml = None
            for t in tl:
                lt = logits.ap()[:, t * NE:(t + 1) * NE]
                pmt = pm.ap()[:, t * NE:(t + 1) * NE]
                ms = (t % 2) * 32
                eq1 = msk.ap()[:, ms:ms + 8]
                l2 = msk.ap()[:, ms + 8:ms + 16]
                eq2 = msk.ap()[:, ms + 16:ms + 24]
                mk = msk.ap()[:, ms + 24:ms + 32]
                exp_toks[t] = P.on("scalar", lambda e, t=t, lt=lt, pmt=pmt: e.activation(
                    out=pmt, in_=lt, func=AF.Exp, bias=stc(NEGM1, t), accum_out=stc(Zc, t)),
                    waits=[t_nm1], inc="S")
                P.on("vector", lambda e, t=t, lt=lt, eq1=eq1: e.tensor_scalar(
                    out=eq1, in0=lt, scalar1=stc(M1, t), scalar2=None, op0=ALU.is_ge),
                    waits=[t_nm1], inc=None, drain=True)
                P.on("vector", lambda e, lt=lt, eq1=eq1, l2=l2: e.scalar_tensor_tensor(
                    out=l2, in0=eq1, scalar=-BIG, in1=lt, op0=ALU.mult, op1=ALU.add),
                    drain=True)
                P.on("vector", lambda e, t=t, l2=l2: e.reduce_max(
                    out=stc(M2c, t), in_=l2, axis=AX.X), drain=True)
                P.on("vector", lambda e, t=t, l2=l2, eq2=eq2: e.tensor_scalar(
                    out=eq2, in0=l2, scalar1=stc(M2c, t), scalar2=None, op0=ALU.is_ge),
                    drain=True)
                P.on("vector", lambda e, eq1=eq1, eq2=eq2, mk=mk: e.tensor_add(
                    out=mk, in0=eq1, in1=eq2), drain=True)
                pml = P.on("vector", lambda e, pmt=pmt, mk=mk: e.tensor_mul(
                    out=pmt, in0=pmt, in1=mk), waits=[exp_toks[t]],
                    inc=("V" if t == tl[-1] else None), drain=True)
            v = lambda fn, waits=(), drain=True: P.on("vector", fn, waits, inc="V", drain=drain)
            g1 = v(lambda e, sl=sl: e.tensor_sub(out=sl(TA), in0=sl(M2c), in1=sl(M1)), [pml])
            g2 = P.on("scalar", lambda e, sl=sl: e.activation(out=sl(E2), in_=sl(TA), func=AF.Exp),
                      waits=[g1, exp_toks[tl[-1]]], inc="S")
            v(lambda e, sl=sl: e.reciprocal(out=sl(RZ), in_=sl(Zc)), [g2], drain=False)
            v(lambda e, sl=sl: e.scalar_tensor_tensor(out=sl(TB), in0=sl(E2), scalar=1.0, in1=sl(RZ),
                                                      op0=ALU.add, op1=ALU.mult))
            v(lambda e, sl=sl: e.tensor_scalar_add(out=sl(TA), in0=sl(TB), scalar1=EPS))
            v(lambda e, sl=sl: e.reciprocal(out=sl(TC), in_=sl(TA)))
            v(lambda e, sl=sl: e.tensor_mul(out=sl(TA), in0=sl(TB), in1=sl(TC)))
            v(lambda e, sl=sl: e.tensor_mul(out=sl(Gc), in0=sl(RZ), in1=sl(TC)))
            v(lambda e, sl=sl: e.tensor_scalar_max(out=sl(TB), in0=sl(TA), scalar1=EPS))
            v(lambda e, sl=sl: e.reciprocal(out=sl(TC), in_=sl(TB)))
            v(lambda e, sl=sl: e.tensor_scalar(out=sl(TA), in0=sl(TA), scalar1=EPS, scalar2=1.0,
                                               op0=ALU.mult, op1=ALU.add))
            v(lambda e, sl=sl: e.tensor_mul(out=sl(TC), in0=sl(TC), in1=sl(TA)))
            v(lambda e, sl=sl: e.tensor_mul(out=sl(Gc), in0=sl(Gc), in1=sl(TC)))
            g14 = v(lambda e, sl=sl: e.tensor_scalar_mul(out=sl(Gc), in0=sl(Gc), scalar1=0.5))
            for t in tl:
                ct[t] = P.on("vector", lambda e, t=t: e.tensor_scalar_mul(
                    out=coef.ap()[:, t * NE:(t + 1) * NE],
                    in0=pm.ap()[:, t * NE:(t + 1) * NE], scalar1=stc(Gc, t)),
                    waits=[g14], inc="V", drain=True)
            return ct

        def emit_ffn(r_deps, xnt_war, bufs, gates, dsems, ph="f", stats_done=None,
                     tail_hook=None, extra_xnscr=None):
            ev = [None] * NT
            ct = {}
            dw_tok = [None] * 4
            mm_last = [[None] * NT for _ in range(4)]
            r_tok = [None] * NT
            hook_out = [None] * NT

            def pair_dma(p, extra):
                buf3 = bufs[p].rearrange("p (e d) -> p e d", e=2 * KC)
                toks = []
                for eeidx in range(2):
                    eidx = 2 * p + eeidx
                    toks.append(P.dma(
                        "gpsimd", buf3[:, eeidx * KC:(eeidx + 1) * KC, :],
                        ew.ap()[eidx].rearrange("(k q) n -> q k n", q=128),
                        dsems[p], waits=list(gates[p]) + extra))
                return toks[-1]

            def experts_tiles(p, tiles):
                for t in tiles:
                    ps = PA if t % 2 == 0 else PB
                    last = None
                    for k in range(KC):
                        w = ([ev[t], dw_tok[p]] + ps_wait(ps, 0, 2048)) if k == 0 else []
                        for ee in range(2):
                            for h in range(2):
                                c0 = ee * 1024 + h * 512
                                stop = (k == KC - 1)
                                fin = (stop and ee == 1 and h == 1)
                                last = P.on("tensor", lambda e, k=k, t=t, ee=ee, h=h, c0=c0,
                                            ps=ps, p=p, stop=stop: e.matmul(
                                    ps.ap()[:, c0:c0 + 512],
                                    XnT.ap()[:, k * TOK + t * 128:k * TOK + (t + 1) * 128],
                                    bufs[p][:, (ee * KC + k) * D + h * 512:
                                             (ee * KC + k) * D + h * 512 + 512],
                                    start=(k == 0), stop=stop),
                                    waits=w, inc=("T" if fin else None))
                                w = []
                    mm_last[p][t] = last
                    P.on("vector", lambda e, t=t, ps=ps, p=p: e.scalar_tensor_tensor(
                        out=Rt(t), in0=ps.ap()[:, 0:1024],
                        scalar=coef.ap()[:, t * NE + 2 * p:t * NE + 2 * p + 1],
                        in1=Rt(t), op0=ALU.mult, op1=ALU.add),
                        waits=[last, ct[t]], inc=None)
                    c1tok = P.on("vector", lambda e, t=t, ps=ps, p=p: e.scalar_tensor_tensor(
                        out=Rt(t), in0=ps.ap()[:, 1024:2048],
                        scalar=coef.ap()[:, t * NE + 2 * p + 1:t * NE + 2 * p + 2],
                        in1=Rt(t), op0=ALU.mult, op1=ALU.add),
                        inc="V", drain=True)
                    ps_set(ps, 0, 2048, c1tok)
                    if p == 3:
                        r_tok[t] = c1tok
                        if tail_hook is not None:
                            hook_out[t] = tail_hook(t, c1tok)

            dw_tok[0] = pair_dma(0, [])
            dw_tok[1] = pair_dma(1, [])
            bm = [None, None]
            for hh in range(2):
                tl = list(range(hh * 8, hh * 8 + 8))
                if stats_done is None:
                    half_stats = [ln_stats_tile(t, r_deps[t], extra_xnscr) for t in tl]
                    toks = [half_stats[-1][0], half_stats[-1][1]]
                else:
                    toks = [stats_done[tl[-1]][0], stats_done[tl[-1]][1]]
                bm[hh] = ln_batch_half(hh, toks)
            for hh in range(2):
                tl = list(range(hh * 8, hh * 8 + 8))
                lg = {}
                for t in tl:
                    ev[t] = ln_norm_tile(t, bm[hh], xnt_war, PA, PAb)
                    lg[t] = router_tile(t, ev[t])
                ct.update(coef_half(hh, lg))
                experts_tiles(0, tl)
            for p in range(1, 4):
                if p >= 2:
                    dw_tok[p] = pair_dma(p, [mm_last[p - 2][NT - 1]])
                experts_tiles(p, range(NT))
            return r_tok, mm_last, hook_out

        t_rbb = P.on("vector", lambda e: e.tensor_copy(out=rbb[:, :], in_=rbf[:, :]),
                     waits=[t_rb], inc="V")
        tr_toks = []
        for k in range(KC):
            tok = P.on("tensor", lambda e, k=k: e.transpose(
                PA.ap()[:, 16 + k * 8:16 + (k + 1) * 8],
                land_rw[0:8, k * 128:(k + 1) * 128], idf[0:8, 0:8]),
                waits=[t_rw, t_idf] + ps_wait(PA, 16, 80), inc=("T" if k == KC - 1 else None))
            if tok:
                tr_toks.append(tok)
        t_wrt = P.on("vector", lambda e: e.tensor_copy(out=WrT[:, :], in_=PA.ap()[:, 16:80]),
                     waits=tr_toks, inc="V")
        ps_set(PA, 16, 80, t_wrt)
        bq_toks = []
        for j in range(8):
            tok = P.on("tensor", lambda e, j=j: e.transpose(
                PA.ap()[:, j:j + 1], QTf0[0:1, j * 128:(j + 1) * 128], idf[0:1, 0:1]),
                waits=[("DBQ", 48), t_idf] + ps_wait(PA, 0, 8), inc=("T" if j == 7 else None))
            if tok:
                bq_toks.append(tok)
        t_bqT = P.on("vector", lambda e: e.tensor_copy(out=bqT[:, :], in_=PA.ap()[:, 0:8]),
                     waits=bq_toks, inc="V")
        ps_set(PA, 0, 8, t_bqT)
        t_bkr = P.on("gpsimd", lambda e: e.tensor_copy(out=bkr[:, :], in_=QTf0[0:1, D:2 * D]),
                     waits=[("DBQ", 48)], inc="G")
        t_bvr = P.on("gpsimd", lambda e: e.tensor_copy(out=bvr[:, :], in_=QTf0[0:1, 2 * D:3 * D]),
                     waits=[("DBQ", 48)], inc="G")
        setup_xnt_tok = t_wrt

        QT3 = QT.ap().rearrange("p (e d) -> p e d", e=2 * KC)
        t_wk = P.dma("gpsimd", QT3[:, 0:KC, :], wk.ap().rearrange("(k q) n -> q k n", q=128), "DK",
                     waits=[t_bqT, t_bkr, t_bvr, ("DX6", 16), ("DX7", 16)])
        t_wv = P.dma("gpsimd", QT3[:, KC:2 * KC, :], wv.ap().rearrange("(k q) n -> q k n", q=128), "DK")
        r1, f1_mms, at_stats = emit_ffn(
            [t_xq[t // 2] for t in range(NT)], setup_xnt_tok, ph="f1",
            bufs=[WA.ap(), WB.ap(), WA.ap(), WB.ap()],
            gates=[[], [], [], []],
            dsems=["DW0", "DW1", "DW0", "DW1"],
            tail_hook=lambda t, tok: ln_stats_tile(t, tok))
        f1_last_mm = f1_mms[3][NT - 1]

        ev2 = [None] * NT
        kv_tok = [None] * NT
        ev_k = [None] * NT
        ev_v = [None] * NT
        bm_at = [None, None]
        for hh in range(2):
            tl = list(range(hh * 8, hh * 8 + 8))
            toks = [at_stats[tl[-1]][0], at_stats[tl[-1]][1]]
            bm_at[hh] = ln_batch_half(hh, toks)
        for hh in range(2):
            tl = list(range(hh * 8, hh * 8 + 8))
            for t in tl:
                ev2[t] = ln_norm_tile(t, bm_at[hh], f1_last_mm if t == 0 else None, PA, PAb)
                kc0 = 1024
                vc0 = (t % 2) * 1024
                last_v = None
                for k in range(KC):
                    w = ([ev2[t], t_wk, t_wv, t_bkr, t_bvr, t_ones1]
                         + ps_wait(PA, kc0, kc0 + 1024) + ps_wait(PB, vc0, vc0 + 1024)) if k == 0 else []
                    for h in range(2):
                        P.on("tensor", lambda e, k=k, t=t, h=h, kc0=kc0: e.matmul(
                            PA.ap()[:, kc0 + h * 512:kc0 + h * 512 + 512],
                            XnT.ap()[:, k * TOK + t * 128:k * TOK + (t + 1) * 128],
                            QT.ap()[:, k * D + h * 512:k * D + h * 512 + 512],
                            start=(k == 0), stop=False), waits=w)
                        w = []
                        P.on("tensor", lambda e, k=k, t=t, h=h, vc0=vc0: e.matmul(
                            PB.ap()[:, vc0 + h * 512:vc0 + h * 512 + 512],
                            XnT.ap()[:, k * TOK + t * 128:k * TOK + (t + 1) * 128],
                            QT.ap()[:, (KC + k) * D + h * 512:(KC + k) * D + h * 512 + 512],
                            start=(k == 0), stop=False))
                for h in range(2):
                    P.on("tensor", lambda e, h=h, kc0=kc0: e.matmul(
                        PA.ap()[:, kc0 + h * 512:kc0 + h * 512 + 512],
                        ones1b[0:1, :], bkr[0:1, h * 512:h * 512 + 512],
                        start=False, stop=True))
                    last_v = P.on("tensor", lambda e, h=h, vc0=vc0: e.matmul(
                        PB.ap()[:, vc0 + h * 512:vc0 + h * 512 + 512],
                        ones1b[0:1, :], bvr[0:1, h * 512:h * 512 + 512],
                        start=False, stop=True), inc=("T" if h == 1 else None))
                kv_tok[t] = last_v
                ev_k[t] = P.on("vector", lambda e, t=t, kc0=kc0: e.tensor_copy(
                    out=WA.ap()[:, t * D:(t + 1) * D], in_=PA.ap()[:, kc0:kc0 + 1024]),
                    waits=[kv_tok[t], f1_mms[2][NT - 1]], inc="V")
                ev_v[t] = P.on("vector", lambda e, t=t, vc0=vc0: e.tensor_copy(
                    out=WB.ap()[:, t * D:(t + 1) * D], in_=PB.ap()[:, vc0:vc0 + 1024]),
                    waits=[kv_tok[t], f1_mms[3][NT - 1]], inc="V")
                ps_set(PA, kc0, kc0 + 1024, ev_k[t])
                ps_set(PB, vc0, vc0 + 1024, ev_v[t])

        m2_dma_toks = []
        m_pass_tok = [None, None]
        for pss in range(2):
            last = None
            for t in range(NT):
                for c in range(4):
                    chunk = pss * 4 + c
                    ps = PA if c < 2 else PB
                    c0 = (c % 2) * 1024
                    w = []
                    if t == 0:
                        w = [ev_k[NT - 1], ev_v[NT - 1]] + ps_wait(ps, c0, c0 + 1024)
                    for h in range(2):
                        stop = (t == NT - 1)
                        last = P.on("tensor", lambda e, t=t, chunk=chunk, h=h, ps=ps, c0=c0,
                                    stop=stop: e.matmul(
                            ps.ap()[:, c0 + h * 512:c0 + h * 512 + 512],
                            WA.ap()[:, t * D + chunk * 128:t * D + (chunk + 1) * 128],
                            WB.ap()[:, t * D + h * 512:t * D + h * 512 + 512],
                            start=(t == 0), stop=stop),
                            waits=w,
                            inc=("T" if (stop and chunk in (3, 7) and h == 1) else None))
                        w = []
            m_pass_tok[pss] = last
            for c in range(4):
                chunk = pss * 4 + c
                ps = PA if c < 2 else PB
                c0 = (c % 2) * 1024
                slot = chunk % NSLOT
                xs = xnscr.ap()[:, slot * D:(slot + 1) * D]
                wr = [m_pass_tok[pss], xnscr_reader[slot], xnscr_writer[slot],
                      ("DM%d" % slot, P.cnt["DM%d" % slot])]
                evt = P.on("vector", lambda e, xs=xs, ps=ps, c0=c0: e.tensor_copy(
                    out=xs, in_=ps.ap()[:, c0:c0 + 1024]), waits=wr, inc="V")
                xnscr_writer[slot] = evt
                m2r = M2.ap()[chunk * 128:(chunk + 1) * 128, :]
                d1 = P.dma("sync", m2r[:, 0:D], xs, "DM%d" % slot, waits=[evt])
                d2 = P.dma("sync", m2r[:, D:2 * D], xs, "DM%d" % slot, waits=[evt])
                sl_src = AP(M2, D + chunk * 128 * (2 * D - 1), [[2 * D - 1, 128], [1, D]])
                P.dma("sync", QT.ap()[:, chunk * D:(chunk + 1) * D], sl_src, "DS",
                      waits=[("DM%d" % slot, P.cnt["DM%d" % slot]), kv_tok[NT - 1]])
                m2_dma_toks += [d1, d2]
                xnscr_reader[slot] = None
                ps_set(ps, c0, c0 + 1024, evt)
        xnscr_dma_gate = None
        m2_all = [("DM%d" % s, P.cnt["DM%d" % s]) for s in range(4)]

        WQ0 = KC * D
        t_wqf = P.dma("gpsimd",
                      QT.ap()[:, WQ0:2 * WQ0].rearrange("p (k n) -> p k n", k=KC),
                      wq.ap().rearrange("(k q) n -> q k n", q=128), "DQ0",
                      waits=[kv_tok[NT - 1]])
        qev = [None] * KC

        def emit_qt_j(j):
            ps = PA if j % 2 == 0 else PB
            last = None
            for k in range(KC):
                w = [t_wqf, ev2[NT - 1]] + (ps_wait(ps, 0, 2048) if k == 0 else [])
                for tb in range(4):
                    stop = (k == KC - 1)
                    fin = (stop and tb == 3)
                    last = P.on("tensor", lambda e, j=j, k=k, tb=tb, ps=ps, stop=stop: e.matmul(
                        ps.ap()[:, tb * 512:(tb + 1) * 512],
                        QT.ap()[:, WQ0 + k * D + j * 128:WQ0 + k * D + (j + 1) * 128],
                        XnT.ap()[:, k * TOK + tb * 512:k * TOK + (tb + 1) * 512],
                        start=(k == 0), stop=stop),
                        waits=w, inc=("T" if fin else None))
                    w = []
            if j % 2 == 0:
                qev[j] = P.on("scalar", lambda e, j=j, ps=ps: e.activation(
                    out=WA.ap()[:, j * TOK:(j + 1) * TOK], in_=ps.ap()[:, 0:2048],
                    func=AF.Identity, bias=bqT[:, j:j + 1], scale=1.0),
                    waits=[last, t_bqT, m_pass_tok[1]], inc="S")
            else:
                qev[j] = P.on("vector", lambda e, j=j, ps=ps: e.tensor_scalar_add(
                    out=WA.ap()[:, j * TOK:(j + 1) * TOK], in0=ps.ap()[:, 0:2048],
                    scalar1=bqT[:, j:j + 1]),
                    waits=[last, t_bqT, m_pass_tok[1]], inc="V")
            ps_set(ps, 0, 2048, qev[j])
            return last
        qT_last_mm = emit_qt_j(0)

        t_slide = ("DS", P.cnt["DS"])
        last = None
        for i in range(KC):
            for h in range(2):
                w = [t_slide, t_onesc] + (ps_wait(PA, 0, 1024) if i == 0 and h == 0 else [])
                stop = (i == KC - 1)
                fin = (stop and h == 1)
                last = P.on("tensor", lambda e, i=i, h=h, stop=stop: e.matmul(
                    PA.ap()[0:1, h * 512:h * 512 + 512], onescol[:, :],
                    QT.ap()[:, i * D + h * 512:i * D + h * 512 + 512],
                    start=(i == 0), stop=stop), waits=w, inc=("T" if fin else None))
        t_ssum = last
        QTf = QT.ap().bitcast(F32)
        s_sb = QTf[0:1, 0:1024]
        t_sev = P.on("vector", lambda e: e.tensor_copy(out=s_sb, in_=PA.ap()[0:1, 0:1024]),
                     waits=[t_ssum], inc="V")
        ps_set(PA, 0, 1024, t_sev)
        t_ccin = P.dma("sync", cc_in.ap(), s_sb, "DS", waits=[t_sev])
        t_cc = P.on("gpsimd", lambda e: e.collective_compute(
            "AllReduce", ALU.add, replica_groups=[[0, 1, 2, 3], [4, 5, 6, 7]],
            ins=[cc_in.ap()], outs=[cc_out.ap()]),
            waits=[("DS", P.cnt["DS"])], inc="CC")
        d_sv1 = P.dma("gpsimd", sv2.ap()[:, 0:D], cc_out.ap(), "DC", waits=[t_cc])
        d_sv2 = P.dma("gpsimd", sv2.ap()[:, D:2 * D], cc_out.ap(), "DC", waits=[t_cc])
        d_sv3 = P.dma("gpsimd", sv2.ap()[:, 2 * D:3 * D], cc_out.ap(), "DC", waits=[t_cc])

        for j in range(1, 8):
            qT_last_mm = emit_qt_j(j)

        csrc = AP(sv2, 0, [[1, 128], [1, 1920]])
        t_C = P.dma("sync", WB.ap()[:, 0:1920], csrc, "DC2",
                    waits=[("DC", P.cnt["DC"]), m_pass_tok[1]])
        rA = [None] * NT
        f2_stats = [None] * NT
        ub_last = None
        for t in range(NT):
            sl4 = t % 4
            ps = PA if sl4 < 2 else PB
            c0 = (sl4 % 2) * 1024
            last = None
            for k in range(KC):
                w = ([t_C, qev[KC - 1]] + ps_wait(ps, c0, c0 + 1024)) if k == 0 else []
                for h in range(2):
                    stop = (k == KC - 1)
                    fin = (stop and h == 1)
                    last = P.on("tensor", lambda e, t=t, k=k, h=h, ps=ps, c0=c0, stop=stop: e.matmul(
                        ps.ap()[:, c0 + h * 512:c0 + h * 512 + 512],
                        WA.ap()[:, k * TOK + t * 128:k * TOK + t * 128 + 128],
                        WB.ap()[:, k * 128 + h * 512:k * 128 + h * 512 + 512],
                        start=(k == 0), stop=stop), waits=w, inc=("T" if fin else None))
                    w = []
            ub_last = last
            rA[t] = P.on("vector", lambda e, t=t, ps=ps, c0=c0: e.tensor_tensor(
                out=Rt(t), in0=ps.ap()[:, c0:c0 + 1024], in1=Rt(t), op=ALU.add),
                waits=[last, r1[t]], inc="V")
            ps_set(ps, c0, c0 + 1024, rA[t])
            f2_stats[t] = ln_stats_tile(t, rA[t], m2_all[t % 4] if t < 4 else None)

        r3, _, _ = emit_ffn(
            rA, qT_last_mm,
            bufs=[QT.ap(), WB.ap(), QT.ap(), WB.ap()],
            gates=[[t_ssum, t_ccin, qT_last_mm], [ub_last], [], []],
            dsems=["DW0", "DW1", "DW0", "DW1"],
            stats_done=f2_stats)

        out3 = out.ap().rearrange("(t p) d -> p t d", p=128)
        t_out = None
        for t in range(NT):
            t_out = P.dma("sync", out3[:, t, :], Rt(t), "DOUT", waits=[r3[t]])
        P.on("sync", lambda e: e.nop(), waits=[t_out])

        P.emit(block)
    return nc


_NC_CACHE = {}


def _get_nc():
    if "nc" not in _NC_CACHE:
        _NC_CACHE["nc"] = build()
    return _NC_CACHE["nc"]


def kernel(L0_attn_x, L0_attn_Wq, L0_attn_bq, L0_attn_Wk, L0_attn_bk,
           L0_attn_Wv, L0_attn_bv, L0_ffn_router_w, L0_ffn_router_b,
           L0_ffn_expert_weights):
    x = np.asarray(L0_attn_x, dtype=np.float32)
    weights = dict(
        wq=np.ascontiguousarray(L0_attn_Wq, dtype=np.float32),
        wk=np.ascontiguousarray(L0_attn_Wk, dtype=np.float32),
        wv=np.ascontiguousarray(L0_attn_Wv, dtype=np.float32),
        bq=np.ascontiguousarray(L0_attn_bq, dtype=np.float32).reshape(1, D),
        bk=np.ascontiguousarray(L0_attn_bk, dtype=np.float32).reshape(1, D),
        bv=np.ascontiguousarray(L0_attn_bv, dtype=np.float32).reshape(1, D),
        rw=np.ascontiguousarray(L0_ffn_router_w, dtype=np.float32),
        rb=np.ascontiguousarray(L0_ffn_router_b, dtype=np.float32).reshape(1, NE),
        ew=np.ascontiguousarray(L0_ffn_expert_weights, dtype=np.float32),
    )
    in_maps = []
    for c in range(NCORES):
        b, s0 = c // 4, (c % 4) * TOK
        m = dict(weights)
        m["x"] = np.ascontiguousarray(x[b, s0:s0 + TOK, :])
        in_maps.append(m)
    nc = _get_nc()
    res = run_bass_kernel_spmd(nc, in_maps, core_ids=list(range(NCORES)))
    out = np.empty((2, 4 * TOK, D), dtype=np.float32)
    for c in range(NCORES):
        b, s0 = c // 4, (c % 4) * TOK
        out[b, s0:s0 + TOK, :] = np.asarray(res.results[c]["out"])
    return out


# version 17
# speedup vs baseline: 1.0170x; 1.0170x over previous
import numpy as np
import concourse.bass as bass
import concourse.mybir as mybir
from concourse.ap import AP
from concourse.bass_utils import run_bass_kernel_spmd

F32 = mybir.dt.float32
BF16 = mybir.dt.bfloat16
AF = mybir.ActivationFunctionType
ALU = mybir.AluOpType
AX = mybir.AxisListType

D = 1024
TOK = 2048
NT = 16
KC = 8
NE = 8
EPS = 1e-10
LNEPS = 1e-5
BIG = 1e30
NCORES = 8

ENGINES = ("tensor", "vector", "scalar", "gpsimd", "sync")


class Prog:

    def __init__(self, nc, es):
        self.nc = nc
        self.es = es
        self.ops = {e: [] for e in ENGINES}
        self.cnt = {}
        self.semh = {}

    def sem(self, name):
        self.semh[name] = self.es.enter_context(self.nc.semaphore(name))
        self.cnt[name] = 0

    def on(self, eng, fn, waits=(), inc=None, amt=1, drain=False):
        tok = None
        if inc is not None:
            self.cnt[inc] += amt
            tok = (inc, self.cnt[inc])
        self.ops[eng].append((fn, tuple(w for w in waits if w is not None), inc, amt, drain))
        return tok

    def dma(self, eng, out, in_, sem, waits=(), **kw):
        return self.on(eng, lambda e, o=out, i=in_, k=kw: e.dma_start(out=o, in_=i, **k),
                       waits, inc=sem, amt=16)

    def emit(self, block):
        for ename in ENGINES:
            ops = self.ops[ename]
            semh = self.semh

            def body(eng, ops=ops):
                hw = {}
                for fn, waits, inc, amt, drain in ops:
                    if drain:
                        eng.drain()
                    byname = {}
                    for (s, v) in waits:
                        byname[s] = max(byname.get(s, 0), v)
                    for s, v in byname.items():
                        if hw.get(s, 0) < v:
                            eng.wait_ge(semh[s], v)
                            hw[s] = v
                    inst = fn(eng)
                    if inc is not None:
                        inst.then_inc(semh[inc], amt)
            getattr(block, ename)(body)


def build():
    nc = bass.Bass("TRN2", target_bir_lowering=False, debug=False)
    x = nc.dram_tensor("x", [TOK, D], F32, kind="ExternalInput")
    wq = nc.dram_tensor("wq", [D, D], F32, kind="ExternalInput")
    wk = nc.dram_tensor("wk", [D, D], F32, kind="ExternalInput")
    wv = nc.dram_tensor("wv", [D, D], F32, kind="ExternalInput")
    bq = nc.dram_tensor("bq", [1, D], F32, kind="ExternalInput")
    bk = nc.dram_tensor("bk", [1, D], F32, kind="ExternalInput")
    bv = nc.dram_tensor("bv", [1, D], F32, kind="ExternalInput")
    rw = nc.dram_tensor("rw", [NE, D], F32, kind="ExternalInput")
    rb = nc.dram_tensor("rb", [1, NE], F32, kind="ExternalInput")
    ew = nc.dram_tensor("ew", [NE, D, D], F32, kind="ExternalInput")
    out = nc.dram_tensor("out", [TOK, D], F32, kind="ExternalOutput")
    M2 = nc.dram_tensor("M2", [D, 2 * D], BF16)
    cc_in = nc.dram_tensor("cc_in", [1, D], F32)
    cc_out = nc.dram_tensor("cc_out", [1, D], F32)
    sv2 = nc.dram_tensor("sv2", [1, 3 * D], BF16)

    from contextlib import ExitStack
    es = ExitStack()
    sb = lambda n, s, dt: es.enter_context(nc.sbuf_tensor(n, s, dt))
    psal = lambda n, s, dt: es.enter_context(nc.psum_tensor(n, s, dt))
    with es:
        R = sb("R", [128, NT * D], F32)
        XnT = sb("XnT", [128, KC * TOK], BF16)
        WA = sb("WA", [128, 2 * KC * D], BF16)
        WB = sb("WB", [128, 2 * KC * D], BF16)
        QT = sb("QT", [128, 2 * KC * D], BF16)
        xnscr = sb("xnscr", [128, 4 * D], BF16)
        NSLOT = 4
        idf = sb("idf", [8, 8], F32)
        idb = sb("idb", [128, 128], BF16)
        ones1b = sb("ones1b", [1, 128], BF16)
        onescol = sb("onescol", [128, 1], BF16)
        WrT = sb("WrT", [128, KC * NE], BF16)
        rbf = sb("rbf", [1, NE], F32)
        rbb = sb("rbb", [1, NE], BF16)
        bqT = sb("bqT", [128, 8], F32)
        bkr = sb("bkr", [1, D], BF16)
        bvr = sb("bvr", [1, D], BF16)
        ST = sb("ST", [128, 256], F32)
        logits = sb("logits", [128, NT * NE], F32)
        pm = sb("pm", [128, NT * NE], F32)
        coef = sb("coef", [128, NT * NE], F32)
        msk = sb("msk", [128, 64], F32)
        PA = psal("PA", [128, 2048], F32)
        PB = psal("PB", [128, 2048], F32)

        def st(i):
            return ST.ap()[:, i * 16:(i + 1) * 16]
        SSUM, SSQ, MU, VAR, RSTD, NEGMUR, M1, NEGM1, M2c, Zc, TA, TB, TC, Gc, E2, RZ = \
            [st(i) for i in range(16)]

        def stc(sl_, t):
            return sl_[:, t:t + 1]

        P = Prog(nc, es)
        for s in ("T", "V", "S", "G", "DX0", "DX1", "DX2", "DX3", "DX4", "DX5", "DX6", "DX7", "DRW", "DRB",
                  "DBQ", "DW0", "DW1", "DK", "DQ0", "DQ1", "DS", "DM0", "DM1", "DM2", "DM3", "DC", "DC2", "CC", "DOUT"):
            P.sem(s)

        block = es.enter_context(nc.Block())

        psum_user = [None] * 8

        def banks_of(tensor, c0, c1):
            base = 0 if tensor is PA else 4
            return list(range(base + c0 // 512, base + (c1 + 511) // 512))

        def ps_wait(tensor, c0, c1):
            return [psum_user[b] for b in banks_of(tensor, c0, c1)]

        def ps_set(tensor, c0, c1, tok):
            for b in banks_of(tensor, c0, c1):
                psum_user[b] = tok

        XnT3 = XnT.ap().rearrange("p (k c) -> p k c", k=KC)
        Rt = lambda t: R.ap()[:, t * D:(t + 1) * D]
        PAb = PA.ap().bitcast(BF16)
        PBb = PB.ap().bitcast(BF16)
        xn_f32 = xnscr.ap().bitcast(F32)

        x3 = x.ap().rearrange("(t p) d -> p t d", p=128)
        R3 = R.ap().rearrange("p (t d) -> p t d", t=NT)
        XnTf = XnT.ap().bitcast(F32)
        land_rw = XnTf[0:8, 0:D]
        t_rw = P.dma("sync", land_rw, rw.ap(), "DRW")
        t_rb = P.dma("sync", rbf[:, :], rb.ap(), "DRB")
        QTf0 = QT.ap().bitcast(F32)
        t_bq = P.dma("scalar", QTf0[0:1, 0:D], bq.ap(), "DBQ")
        t_bk = P.dma("scalar", QTf0[0:1, D:2 * D], bk.ap(), "DBQ")
        t_bv = P.dma("scalar", QTf0[0:1, 2 * D:3 * D], bv.ap(), "DBQ")
        t_xq = [P.dma("sync" if q % 2 == 0 else "scalar",
                      R3[:, q * 2:(q + 1) * 2, :], x3[:, q * 2:(q + 1) * 2, :], "DX%d" % q,
                      waits=())
                for q in range(8)]
        t_ones1 = P.on("gpsimd", lambda e: e.memset(ones1b[:, :], 1.0), inc="G")
        t_onesc = P.on("gpsimd", lambda e: e.memset(onescol[:, :], 1.0), inc="G")
        t_idsrc = P.on("gpsimd", lambda e: e.memset(xn_f32[:, 0:128], 1.0), inc="G")
        t_idsrcb = P.on("gpsimd", lambda e: e.memset(xnscr.ap()[:, 0:128], 1.0), inc="G", drain=True)
        t_idf = P.on("gpsimd", lambda e: e.affine_select(
            out=idf[:, :], in_=xn_f32[0:8, 64:72], pattern=[[-1, 8]],
            compare_op=ALU.is_equal, fill=0.0, base=0, channel_multiplier=1),
            waits=[t_idsrc], inc="G", drain=True)
        t_idb0 = P.on("gpsimd", lambda e: e.memset(idb[:, :], 0.0), inc="G")
        t_idb = P.on("gpsimd", lambda e: e.affine_select(
            out=idb[:, :], in_=xnscr.ap()[:, 0:128], pattern=[[-1, 128]],
            compare_op=ALU.is_equal, fill=0.0, base=0, channel_multiplier=1),
            waits=[t_idb0, t_idsrcb], inc="G", drain=True)
        xnscr_reader = [t_idb, t_idb, t_idb, t_idb]
        xnscr_writer = [t_idsrcb, t_idsrcb, t_idsrcb, t_idsrcb]

        def ln_stats_tile(t, r_dep, extra=None):
            slot = t % NSLOT
            xs = xnscr.ap()[:, slot * D:(slot + 1) * D]
            sq = P.on("scalar", lambda e, xs=xs, t=t: e.activation(
                out=xs, in_=Rt(t), func=AF.Square, accum_out=stc(SSQ, t)),
                waits=[r_dep, xnscr_reader[slot], xnscr_writer[slot], extra], inc="S")
            xnscr_writer[slot] = sq
            xnscr_reader[slot] = None
            sm = P.on("vector", lambda e, t=t: e.reduce_sum(
                out=stc(SSUM, t), in_=Rt(t), axis=AX.X),
                waits=[r_dep], inc="V")
            return sq, sm

        def ln_batch_half(hh, stat_toks):
            sl = lambda S: S[:, hh * 8:hh * 8 + 8]
            v = lambda fn, waits=(), drain=True: P.on("vector", fn, waits, inc="V", drain=drain)
            t1 = v(lambda e, sl=sl: e.tensor_scalar_mul(out=sl(TA), in0=sl(SSQ), scalar1=1.0 / D),
                   stat_toks, drain=False)
            t2 = v(lambda e, sl=sl: e.tensor_scalar_mul(out=sl(MU), in0=sl(SSUM), scalar1=1.0 / D))
            t3 = v(lambda e, sl=sl: e.tensor_mul(out=sl(TB), in0=sl(MU), in1=sl(MU)), [t2])
            t4 = v(lambda e, sl=sl: e.tensor_sub(out=sl(VAR), in0=sl(TA), in1=sl(TB)), [t1, t3])
            t5 = v(lambda e, sl=sl: e.tensor_scalar_add(out=sl(VAR), in0=sl(VAR), scalar1=LNEPS), [t4])
            t6 = P.on("scalar", lambda e, sl=sl: e.activation(out=sl(TA), in_=sl(VAR), func=AF.Ln),
                      waits=[t5], inc="S")
            pr = P.on("scalar", lambda e, sl=sl: e.activation(out=sl(RSTD), in_=sl(TA),
                                                              func=AF.Exp, scale=-0.5),
                      waits=[t6], inc="S", drain=True)
            m1t = v(lambda e, sl=sl: e.tensor_mul(out=sl(TA), in0=sl(MU), in1=sl(RSTD)), [pr])
            t_nm = v(lambda e, sl=sl: e.tensor_scalar_mul(out=sl(NEGMUR), in0=sl(TA), scalar1=-1.0), [m1t])
            return t_nm

        def ln_norm_tile(t, bm_tok, xnt_war, tr_ps, tr_psb):
            slot = t % NSLOT
            xs = xnscr.ap()[:, slot * D:(slot + 1) * D]
            nrm = P.on("scalar", lambda e, xs=xs, t=t: e.activation(
                out=xs, in_=Rt(t), func=AF.Identity, bias=stc(NEGMUR, t), scale=stc(RSTD, t)),
                waits=[bm_tok, xnscr_reader[slot], xnscr_writer[slot]], inc="S")
            xnscr_writer[slot] = nrm
            bf0 = (t % 2) * 1024
            trs = None
            for j in range(KC):
                w = [nrm, t_idb] + (ps_wait(tr_ps, (t % 2) * 512, (t % 2 + 1) * 512) if j == 0 else [])
                if j == 0 and xnt_war is not None:
                    w.append(xnt_war)
                trs = P.on("tensor", lambda e, xs=xs, j=j, bf0=bf0, tr_psb=tr_psb: e.transpose(
                    tr_psb[:, bf0 + j * 128:bf0 + (j + 1) * 128],
                    xs[:, j * 128:(j + 1) * 128], idb[:, :]),
                    waits=w, inc=("T" if j == KC - 1 else None))
            xnscr_reader[slot] = trs
            evtok = P.on("vector", lambda e, t=t, bf0=bf0, tr_psb=tr_psb: e.tensor_copy(
                out=XnT3[:, :, t * 128:(t + 1) * 128],
                in_=tr_psb[:, bf0:bf0 + 1024].rearrange("p (j c) -> p j c", j=8)),
                waits=[trs], inc="V")
            ps_set(tr_ps, (t % 2) * 512, (t % 2 + 1) * 512, evtok)
            return evtok

        def router_tile(t, ev_t):
            pcol = (t % 2) * 512
            pslot = PB.ap()[:, pcol:pcol + 8]
            for k in range(KC):
                w = ([ev_t, t_wrt, t_rbb, t_ones1] + ps_wait(PB, pcol, pcol + 8)) if k == 0 else []
                P.on("tensor", lambda e, k=k, t=t, pslot=pslot: e.matmul(
                    pslot, XnT.ap()[:, k * TOK + t * 128:k * TOK + (t + 1) * 128],
                    WrT[:, k * NE:(k + 1) * NE], start=(k == 0), stop=False), waits=w)
            rt = P.on("tensor", lambda e, pslot=pslot: e.matmul(
                pslot, ones1b[0:1, :], rbb[0:1, :], start=False, stop=True), inc="T")
            lg = P.on("vector", lambda e, t=t, pslot=pslot: e.tensor_copy(
                out=logits.ap()[:, t * NE:(t + 1) * NE], in_=pslot),
                waits=[rt], inc="V")
            ps_set(PB, pcol, pcol + 8, lg)
            return lg

        def coef_half(hh, lg):
            tl = list(range(hh * 8, hh * 8 + 8))
            sl = lambda S: S[:, hh * 8:hh * 8 + 8]
            ct = {}
            m1l = None
            for t in tl:
                lt = logits.ap()[:, t * NE:(t + 1) * NE]
                m1l = P.on("vector", lambda e, t=t, lt=lt: e.reduce_max(
                    out=stc(M1, t), in_=lt, axis=AX.X), waits=[lg[t]],
                    inc=("V" if t == tl[-1] else None))
            t_nm1 = P.on("vector", lambda e, sl=sl: e.tensor_scalar_mul(
                out=sl(NEGM1), in0=sl(M1), scalar1=-1.0), waits=[m1l], inc="V", drain=True)
            exp_toks = {}
            pml = None
            for t in tl:
                lt = logits.ap()[:, t * NE:(t + 1) * NE]
                pmt = pm.ap()[:, t * NE:(t + 1) * NE]
                ms = (t % 2) * 32
                eq1 = msk.ap()[:, ms:ms + 8]
                l2 = msk.ap()[:, ms + 8:ms + 16]
                eq2 = msk.ap()[:, ms + 16:ms + 24]
                mk = msk.ap()[:, ms + 24:ms + 32]
                exp_toks[t] = P.on("scalar", lambda e, t=t, lt=lt, pmt=pmt: e.activation(
                    out=pmt, in_=lt, func=AF.Exp, bias=stc(NEGM1, t), accum_out=stc(Zc, t)),
                    waits=[t_nm1], inc="S")
                P.on("vector", lambda e, t=t, lt=lt, eq1=eq1: e.tensor_scalar(
                    out=eq1, in0=lt, scalar1=stc(M1, t), scalar2=None, op0=ALU.is_ge),
                    waits=[t_nm1], inc=None, drain=True)
                P.on("vector", lambda e, lt=lt, eq1=eq1, l2=l2: e.scalar_tensor_tensor(
                    out=l2, in0=eq1, scalar=-BIG, in1=lt, op0=ALU.mult, op1=ALU.add),
                    drain=True)
                P.on("vector", lambda e, t=t, l2=l2: e.reduce_max(
                    out=stc(M2c, t), in_=l2, axis=AX.X), drain=True)
                P.on("vector", lambda e, t=t, l2=l2, eq2=eq2: e.tensor_scalar(
                    out=eq2, in0=l2, scalar1=stc(M2c, t), scalar2=None, op0=ALU.is_ge),
                    drain=True)
                P.on("vector", lambda e, eq1=eq1, eq2=eq2, mk=mk: e.tensor_add(
                    out=mk, in0=eq1, in1=eq2), drain=True)
                pml = P.on("vector", lambda e, pmt=pmt, mk=mk: e.tensor_mul(
                    out=pmt, in0=pmt, in1=mk), waits=[exp_toks[t]],
                    inc=("V" if t == tl[-1] else None), drain=True)
            v = lambda fn, waits=(), drain=True: P.on("vector", fn, waits, inc="V", drain=drain)
            g1 = v(lambda e, sl=sl: e.tensor_sub(out=sl(TA), in0=sl(M2c), in1=sl(M1)), [pml])
            g2 = P.on("scalar", lambda e, sl=sl: e.activation(out=sl(E2), in_=sl(TA), func=AF.Exp),
                      waits=[g1, exp_toks[tl[-1]]], inc="S")
            v(lambda e, sl=sl: e.reciprocal(out=sl(RZ), in_=sl(Zc)), [g2], drain=False)
            v(lambda e, sl=sl: e.scalar_tensor_tensor(out=sl(TB), in0=sl(E2), scalar=1.0, in1=sl(RZ),
                                                      op0=ALU.add, op1=ALU.mult))
            v(lambda e, sl=sl: e.tensor_scalar_add(out=sl(TA), in0=sl(TB), scalar1=EPS))
            v(lambda e, sl=sl: e.reciprocal(out=sl(TC), in_=sl(TA)))
            v(lambda e, sl=sl: e.tensor_mul(out=sl(TA), in0=sl(TB), in1=sl(TC)))
            v(lambda e, sl=sl: e.tensor_mul(out=sl(Gc), in0=sl(RZ), in1=sl(TC)))
            v(lambda e, sl=sl: e.tensor_scalar_max(out=sl(TB), in0=sl(TA), scalar1=EPS))
            v(lambda e, sl=sl: e.reciprocal(out=sl(TC), in_=sl(TB)))
            v(lambda e, sl=sl: e.tensor_scalar(out=sl(TA), in0=sl(TA), scalar1=EPS, scalar2=1.0,
                                               op0=ALU.mult, op1=ALU.add))
            v(lambda e, sl=sl: e.tensor_mul(out=sl(TC), in0=sl(TC), in1=sl(TA)))
            v(lambda e, sl=sl: e.tensor_mul(out=sl(Gc), in0=sl(Gc), in1=sl(TC)))
            g14 = v(lambda e, sl=sl: e.tensor_scalar_mul(out=sl(Gc), in0=sl(Gc), scalar1=0.5))
            for t in tl:
                ct[t] = P.on("vector", lambda e, t=t: e.tensor_scalar_mul(
                    out=coef.ap()[:, t * NE:(t + 1) * NE],
                    in0=pm.ap()[:, t * NE:(t + 1) * NE], scalar1=stc(Gc, t)),
                    waits=[g14], inc="V", drain=True)
            return ct

        def emit_ffn(r_deps, xnt_war, bufs, gates, dsems, ph="f", stats_done=None,
                     tail_hook=None, extra_xnscr=None, bm_pre=None):
            ev = [None] * NT
            ct = {}
            dw_tok = [None] * 4
            mm_last = [[None] * NT for _ in range(4)]
            r_tok = [None] * NT
            hook_out = [None] * NT

            def pair_dma(p, extra):
                buf3 = bufs[p].rearrange("p (e d) -> p e d", e=2 * KC)
                toks = []
                for eeidx in range(2):
                    eidx = 2 * p + eeidx
                    toks.append(P.dma(
                        "gpsimd", buf3[:, eeidx * KC:(eeidx + 1) * KC, :],
                        ew.ap()[eidx].rearrange("(k q) n -> q k n", q=128),
                        dsems[p], waits=list(gates[p]) + extra))
                return toks[-1]

            def experts_tiles(p, tiles):
                for t in tiles:
                    ps = PA if t % 2 == 0 else PB
                    last = None
                    for k in range(KC):
                        w = ([ev[t], dw_tok[p]] + ps_wait(ps, 0, 2048)) if k == 0 else []
                        for ee in range(2):
                            for h in range(2):
                                c0 = ee * 1024 + h * 512
                                stop = (k == KC - 1)
                                fin = (stop and ee == 1 and h == 1)
                                last = P.on("tensor", lambda e, k=k, t=t, ee=ee, h=h, c0=c0,
                                            ps=ps, p=p, stop=stop: e.matmul(
                                    ps.ap()[:, c0:c0 + 512],
                                    XnT.ap()[:, k * TOK + t * 128:k * TOK + (t + 1) * 128],
                                    bufs[p][:, (ee * KC + k) * D + h * 512:
                                             (ee * KC + k) * D + h * 512 + 512],
                                    start=(k == 0), stop=stop),
                                    waits=w, inc=("T" if fin else None))
                                w = []
                    mm_last[p][t] = last
                    P.on("vector", lambda e, t=t, ps=ps, p=p: e.scalar_tensor_tensor(
                        out=Rt(t), in0=ps.ap()[:, 0:1024],
                        scalar=coef.ap()[:, t * NE + 2 * p:t * NE + 2 * p + 1],
                        in1=Rt(t), op0=ALU.mult, op1=ALU.add),
                        waits=[last, ct[t]], inc=None)
                    c1tok = P.on("vector", lambda e, t=t, ps=ps, p=p: e.scalar_tensor_tensor(
                        out=Rt(t), in0=ps.ap()[:, 1024:2048],
                        scalar=coef.ap()[:, t * NE + 2 * p + 1:t * NE + 2 * p + 2],
                        in1=Rt(t), op0=ALU.mult, op1=ALU.add),
                        inc="V", drain=True)
                    ps_set(ps, 0, 2048, c1tok)
                    if p == 3:
                        r_tok[t] = c1tok
                        if tail_hook is not None:
                            hook_out[t] = tail_hook(t, c1tok)

            dw_tok[0] = pair_dma(0, [])
            dw_tok[1] = pair_dma(1, [])
            bm = [None, None]
            for hh in range(2):
                if bm_pre is not None:
                    bm[hh] = bm_pre[hh]
                    continue
                tl = list(range(hh * 8, hh * 8 + 8))
                if stats_done is None:
                    half_stats = [ln_stats_tile(t, r_deps[t], extra_xnscr) for t in tl]
                    toks = [half_stats[-1][0], half_stats[-1][1]]
                else:
                    toks = [stats_done[tl[-1]][0], stats_done[tl[-1]][1]]
                bm[hh] = ln_batch_half(hh, toks)
            for hh in range(2):
                tl = list(range(hh * 8, hh * 8 + 8))
                lg = {}
                for t in tl:
                    ev[t] = ln_norm_tile(t, bm[hh], xnt_war, PA, PAb)
                    lg[t] = router_tile(t, ev[t])
                ct.update(coef_half(hh, lg))
                experts_tiles(0, tl)
            for p in range(1, 4):
                if p >= 2:
                    dw_tok[p] = pair_dma(p, [mm_last[p - 2][NT - 1]])
                experts_tiles(p, range(NT))
            return r_tok, mm_last, hook_out

        t_rbb = P.on("vector", lambda e: e.tensor_copy(out=rbb[:, :], in_=rbf[:, :]),
                     waits=[t_rb], inc="V")
        tr_toks = []
        for k in range(KC):
            tok = P.on("tensor", lambda e, k=k: e.transpose(
                PA.ap()[:, 16 + k * 8:16 + (k + 1) * 8],
                land_rw[0:8, k * 128:(k + 1) * 128], idf[0:8, 0:8]),
                waits=[t_rw, t_idf] + ps_wait(PA, 16, 80), inc=("T" if k == KC - 1 else None))
            if tok:
                tr_toks.append(tok)
        t_wrt = P.on("vector", lambda e: e.tensor_copy(out=WrT[:, :], in_=PA.ap()[:, 16:80]),
                     waits=tr_toks, inc="V")
        ps_set(PA, 16, 80, t_wrt)
        bq_toks = []
        for j in range(8):
            tok = P.on("tensor", lambda e, j=j: e.transpose(
                PA.ap()[:, j:j + 1], QTf0[0:1, j * 128:(j + 1) * 128], idf[0:1, 0:1]),
                waits=[("DBQ", 48), t_idf] + ps_wait(PA, 0, 8), inc=("T" if j == 7 else None))
            if tok:
                bq_toks.append(tok)
        t_bqT = P.on("vector", lambda e: e.tensor_copy(out=bqT[:, :], in_=PA.ap()[:, 0:8]),
                     waits=bq_toks, inc="V")
        ps_set(PA, 0, 8, t_bqT)
        t_bkr = P.on("gpsimd", lambda e: e.tensor_copy(out=bkr[:, :], in_=QTf0[0:1, D:2 * D]),
                     waits=[("DBQ", 48)], inc="G")
        t_bvr = P.on("gpsimd", lambda e: e.tensor_copy(out=bvr[:, :], in_=QTf0[0:1, 2 * D:3 * D]),
                     waits=[("DBQ", 48)], inc="G")
        setup_xnt_tok = t_wrt

        QT3 = QT.ap().rearrange("p (e d) -> p e d", e=2 * KC)
        t_wk = P.dma("gpsimd", QT3[:, 0:KC, :], wk.ap().rearrange("(k q) n -> q k n", q=128), "DK",
                     waits=[t_bqT, t_bkr, t_bvr, ("DX6", 16), ("DX7", 16)])
        t_wv = P.dma("gpsimd", QT3[:, KC:2 * KC, :], wv.ap().rearrange("(k q) n -> q k n", q=128), "DK")
        bm_at = [None, None]

        def f1_hook(t, tok):
            st_ = ln_stats_tile(t, tok)
            if t == 7:
                bm_at[0] = ln_batch_half(0, list(st_))
            elif t == 15:
                bm_at[1] = ln_batch_half(1, list(st_))
            return st_

        r1, f1_mms, at_stats = emit_ffn(
            [t_xq[t // 2] for t in range(NT)], setup_xnt_tok, ph="f1",
            bufs=[WA.ap(), WB.ap(), WA.ap(), WB.ap()],
            gates=[[], [], [], []],
            dsems=["DW0", "DW1", "DW0", "DW1"],
            tail_hook=f1_hook)
        f1_last_mm = f1_mms[3][NT - 1]

        ev2 = [None] * NT
        kv_tok = [None] * NT
        ev_k = [None] * NT
        ev_v = [None] * NT
        for hh in range(2):
            tl = list(range(hh * 8, hh * 8 + 8))
            for t in tl:
                ev2[t] = ln_norm_tile(t, bm_at[hh], f1_last_mm if t == 0 else None, PA, PAb)
                kc0 = 1024
                vc0 = (t % 2) * 1024
                last_v = None
                for k in range(KC):
                    w = ([ev2[t], t_wk, t_wv, t_bkr, t_bvr, t_ones1]
                         + ps_wait(PA, kc0, kc0 + 1024) + ps_wait(PB, vc0, vc0 + 1024)) if k == 0 else []
                    for h in range(2):
                        P.on("tensor", lambda e, k=k, t=t, h=h, kc0=kc0: e.matmul(
                            PA.ap()[:, kc0 + h * 512:kc0 + h * 512 + 512],
                            XnT.ap()[:, k * TOK + t * 128:k * TOK + (t + 1) * 128],
                            QT.ap()[:, k * D + h * 512:k * D + h * 512 + 512],
                            start=(k == 0), stop=False), waits=w)
                        w = []
                        P.on("tensor", lambda e, k=k, t=t, h=h, vc0=vc0: e.matmul(
                            PB.ap()[:, vc0 + h * 512:vc0 + h * 512 + 512],
                            XnT.ap()[:, k * TOK + t * 128:k * TOK + (t + 1) * 128],
                            QT.ap()[:, (KC + k) * D + h * 512:(KC + k) * D + h * 512 + 512],
                            start=(k == 0), stop=False))
                for h in range(2):
                    P.on("tensor", lambda e, h=h, kc0=kc0: e.matmul(
                        PA.ap()[:, kc0 + h * 512:kc0 + h * 512 + 512],
                        ones1b[0:1, :], bkr[0:1, h * 512:h * 512 + 512],
                        start=False, stop=True))
                    last_v = P.on("tensor", lambda e, h=h, vc0=vc0: e.matmul(
                        PB.ap()[:, vc0 + h * 512:vc0 + h * 512 + 512],
                        ones1b[0:1, :], bvr[0:1, h * 512:h * 512 + 512],
                        start=False, stop=True), inc=("T" if h == 1 else None))
                kv_tok[t] = last_v
                ev_k[t] = P.on("vector", lambda e, t=t, kc0=kc0: e.tensor_copy(
                    out=WA.ap()[:, t * D:(t + 1) * D], in_=PA.ap()[:, kc0:kc0 + 1024]),
                    waits=[kv_tok[t], f1_mms[2][NT - 1]], inc="V")
                ev_v[t] = P.on("vector", lambda e, t=t, vc0=vc0: e.tensor_copy(
                    out=WB.ap()[:, t * D:(t + 1) * D], in_=PB.ap()[:, vc0:vc0 + 1024]),
                    waits=[kv_tok[t], f1_mms[3][NT - 1]], inc="V")
                ps_set(PA, kc0, kc0 + 1024, ev_k[t])
                ps_set(PB, vc0, vc0 + 1024, ev_v[t])

        m2_dma_toks = []
        m_pass_tok = [None, None]
        for pss in range(2):
            last = None
            for t in range(NT):
                for c in range(4):
                    chunk = pss * 4 + c
                    ps = PA if c < 2 else PB
                    c0 = (c % 2) * 1024
                    w = []
                    if t == 0:
                        w = [ev_k[NT - 1], ev_v[NT - 1]] + ps_wait(ps, c0, c0 + 1024)
                    for h in range(2):
                        stop = (t == NT - 1)
                        last = P.on("tensor", lambda e, t=t, chunk=chunk, h=h, ps=ps, c0=c0,
                                    stop=stop: e.matmul(
                            ps.ap()[:, c0 + h * 512:c0 + h * 512 + 512],
                            WA.ap()[:, t * D + chunk * 128:t * D + (chunk + 1) * 128],
                            WB.ap()[:, t * D + h * 512:t * D + h * 512 + 512],
                            start=(t == 0), stop=stop),
                            waits=w,
                            inc=("T" if (stop and chunk in (3, 7) and h == 1) else None))
                        w = []
            m_pass_tok[pss] = last
            for c in range(4):
                chunk = pss * 4 + c
                ps = PA if c < 2 else PB
                c0 = (c % 2) * 1024
                slot = chunk % NSLOT
                xs = xnscr.ap()[:, slot * D:(slot + 1) * D]
                wr = [m_pass_tok[pss], xnscr_reader[slot], xnscr_writer[slot],
                      ("DM%d" % slot, P.cnt["DM%d" % slot])]
                evt = P.on("vector", lambda e, xs=xs, ps=ps, c0=c0: e.tensor_copy(
                    out=xs, in_=ps.ap()[:, c0:c0 + 1024]), waits=wr, inc="V")
                xnscr_writer[slot] = evt
                m2r = M2.ap()[chunk * 128:(chunk + 1) * 128, :]
                d1 = P.dma("sync", m2r[:, 0:D], xs, "DM%d" % slot, waits=[evt])
                d2 = P.dma("sync", m2r[:, D:2 * D], xs, "DM%d" % slot, waits=[evt])
                sl_src = AP(M2, D + chunk * 128 * (2 * D - 1), [[2 * D - 1, 128], [1, D]])
                P.dma("sync", QT.ap()[:, chunk * D:(chunk + 1) * D], sl_src, "DS",
                      waits=[("DM%d" % slot, P.cnt["DM%d" % slot]), kv_tok[NT - 1]])
                m2_dma_toks += [d1, d2]
                xnscr_reader[slot] = None
                ps_set(ps, c0, c0 + 1024, evt)
        xnscr_dma_gate = None
        m2_all = [("DM%d" % s, P.cnt["DM%d" % s]) for s in range(4)]

        WQ0 = KC * D
        t_wqf = P.dma("gpsimd",
                      QT.ap()[:, WQ0:2 * WQ0].rearrange("p (k n) -> p k n", k=KC),
                      wq.ap().rearrange("(k q) n -> q k n", q=128), "DQ0",
                      waits=[kv_tok[NT - 1]])
        qev = [None] * KC

        def emit_qt_j(j):
            ps = PA if j % 2 == 0 else PB
            last = None
            for k in range(KC):
                w = [t_wqf, ev2[NT - 1]] + (ps_wait(ps, 0, 2048) if k == 0 else [])
                for tb in range(4):
                    stop = (k == KC - 1)
                    fin = (stop and tb == 3)
                    last = P.on("tensor", lambda e, j=j, k=k, tb=tb, ps=ps, stop=stop: e.matmul(
                        ps.ap()[:, tb * 512:(tb + 1) * 512],
                        QT.ap()[:, WQ0 + k * D + j * 128:WQ0 + k * D + (j + 1) * 128],
                        XnT.ap()[:, k * TOK + tb * 512:k * TOK + (tb + 1) * 512],
                        start=(k == 0), stop=stop),
                        waits=w, inc=("T" if fin else None))
                    w = []
            if j % 2 == 0:
                qev[j] = P.on("scalar", lambda e, j=j, ps=ps: e.activation(
                    out=WA.ap()[:, j * TOK:(j + 1) * TOK], in_=ps.ap()[:, 0:2048],
                    func=AF.Identity, bias=bqT[:, j:j + 1], scale=1.0),
                    waits=[last, t_bqT, m_pass_tok[1]], inc="S")
            else:
                qev[j] = P.on("vector", lambda e, j=j, ps=ps: e.tensor_scalar_add(
                    out=WA.ap()[:, j * TOK:(j + 1) * TOK], in0=ps.ap()[:, 0:2048],
                    scalar1=bqT[:, j:j + 1]),
                    waits=[last, t_bqT, m_pass_tok[1]], inc="V")
            ps_set(ps, 0, 2048, qev[j])
            return last
        qT_last_mm = emit_qt_j(0)

        t_slide = ("DS", P.cnt["DS"])
        last = None
        for i in range(KC):
            for h in range(2):
                w = [t_slide, t_onesc] + (ps_wait(PA, 0, 1024) if i == 0 and h == 0 else [])
                stop = (i == KC - 1)
                fin = (stop and h == 1)
                last = P.on("tensor", lambda e, i=i, h=h, stop=stop: e.matmul(
                    PA.ap()[0:1, h * 512:h * 512 + 512], onescol[:, :],
                    QT.ap()[:, i * D + h * 512:i * D + h * 512 + 512],
                    start=(i == 0), stop=stop), waits=w, inc=("T" if fin else None))
        t_ssum = last
        QTf = QT.ap().bitcast(F32)
        s_sb = QTf[0:1, 0:1024]
        t_sev = P.on("vector", lambda e: e.tensor_copy(out=s_sb, in_=PA.ap()[0:1, 0:1024]),
                     waits=[t_ssum], inc="V")
        ps_set(PA, 0, 1024, t_sev)
        t_ccin = P.dma("sync", cc_in.ap(), s_sb, "DS", waits=[t_sev])
        t_cc = P.on("gpsimd", lambda e: e.collective_compute(
            "AllReduce", ALU.add, replica_groups=[[0, 1, 2, 3], [4, 5, 6, 7]],
            ins=[cc_in.ap()], outs=[cc_out.ap()]),
            waits=[("DS", P.cnt["DS"])], inc="CC")
        d_sv1 = P.dma("gpsimd", sv2.ap()[:, 0:D], cc_out.ap(), "DC", waits=[t_cc])
        d_sv2 = P.dma("gpsimd", sv2.ap()[:, D:2 * D], cc_out.ap(), "DC", waits=[t_cc])
        d_sv3 = P.dma("gpsimd", sv2.ap()[:, 2 * D:3 * D], cc_out.ap(), "DC", waits=[t_cc])

        for j in range(1, 8):
            qT_last_mm = emit_qt_j(j)

        csrc = AP(sv2, 0, [[1, 128], [1, 1920]])
        t_C = P.dma("sync", WB.ap()[:, 0:1920], csrc, "DC2",
                    waits=[("DC", P.cnt["DC"]), m_pass_tok[1]])
        rA = [None] * NT
        f2_stats = [None] * NT
        f2_bm = [None, None]
        ub_last = None
        for t in range(NT):
            sl4 = t % 4
            ps = PA if sl4 < 2 else PB
            c0 = (sl4 % 2) * 1024
            last = None
            for k in range(KC):
                w = ([t_C, qev[KC - 1]] + ps_wait(ps, c0, c0 + 1024)) if k == 0 else []
                for h in range(2):
                    stop = (k == KC - 1)
                    fin = (stop and h == 1)
                    last = P.on("tensor", lambda e, t=t, k=k, h=h, ps=ps, c0=c0, stop=stop: e.matmul(
                        ps.ap()[:, c0 + h * 512:c0 + h * 512 + 512],
                        WA.ap()[:, k * TOK + t * 128:k * TOK + t * 128 + 128],
                        WB.ap()[:, k * 128 + h * 512:k * 128 + h * 512 + 512],
                        start=(k == 0), stop=stop), waits=w, inc=("T" if fin else None))
                    w = []
            ub_last = last
            rA[t] = P.on("vector", lambda e, t=t, ps=ps, c0=c0: e.tensor_tensor(
                out=Rt(t), in0=ps.ap()[:, c0:c0 + 1024], in1=Rt(t), op=ALU.add),
                waits=[last, r1[t]], inc="V")
            ps_set(ps, c0, c0 + 1024, rA[t])
            f2_stats[t] = ln_stats_tile(t, rA[t], m2_all[t % 4] if t < 4 else None)
            if t == 7:
                f2_bm[0] = ln_batch_half(0, list(f2_stats[t]))
            elif t == 15:
                f2_bm[1] = ln_batch_half(1, list(f2_stats[t]))

        r3, _, _ = emit_ffn(
            rA, qT_last_mm,
            bufs=[QT.ap(), WB.ap(), QT.ap(), WB.ap()],
            gates=[[t_ssum, t_ccin, qT_last_mm], [ub_last], [], []],
            dsems=["DW0", "DW1", "DW0", "DW1"],
            stats_done=f2_stats, bm_pre=f2_bm)

        out3 = out.ap().rearrange("(t p) d -> p t d", p=128)
        t_out = None
        for t in range(NT):
            t_out = P.dma("sync", out3[:, t, :], Rt(t), "DOUT", waits=[r3[t]])
        P.on("sync", lambda e: e.nop(), waits=[t_out])

        P.emit(block)
    return nc


_NC_CACHE = {}


def _get_nc():
    if "nc" not in _NC_CACHE:
        _NC_CACHE["nc"] = build()
    return _NC_CACHE["nc"]


def kernel(L0_attn_x, L0_attn_Wq, L0_attn_bq, L0_attn_Wk, L0_attn_bk,
           L0_attn_Wv, L0_attn_bv, L0_ffn_router_w, L0_ffn_router_b,
           L0_ffn_expert_weights):
    x = np.asarray(L0_attn_x, dtype=np.float32)
    weights = dict(
        wq=np.ascontiguousarray(L0_attn_Wq, dtype=np.float32),
        wk=np.ascontiguousarray(L0_attn_Wk, dtype=np.float32),
        wv=np.ascontiguousarray(L0_attn_Wv, dtype=np.float32),
        bq=np.ascontiguousarray(L0_attn_bq, dtype=np.float32).reshape(1, D),
        bk=np.ascontiguousarray(L0_attn_bk, dtype=np.float32).reshape(1, D),
        bv=np.ascontiguousarray(L0_attn_bv, dtype=np.float32).reshape(1, D),
        rw=np.ascontiguousarray(L0_ffn_router_w, dtype=np.float32),
        rb=np.ascontiguousarray(L0_ffn_router_b, dtype=np.float32).reshape(1, NE),
        ew=np.ascontiguousarray(L0_ffn_expert_weights, dtype=np.float32),
    )
    in_maps = []
    for c in range(NCORES):
        b, s0 = c // 4, (c % 4) * TOK
        m = dict(weights)
        m["x"] = np.ascontiguousarray(x[b, s0:s0 + TOK, :])
        in_maps.append(m)
    nc = _get_nc()
    res = run_bass_kernel_spmd(nc, in_maps, core_ids=list(range(NCORES)))
    out = np.empty((2, 4 * TOK, D), dtype=np.float32)
    for c in range(NCORES):
        b, s0 = c // 4, (c % 4) * TOK
        out[b, s0:s0 + TOK, :] = np.asarray(res.results[c]["out"])
    return out
